# version 28
# baseline (speedup 1.0000x reference)
import numpy as np
import ml_dtypes
from contextlib import ExitStack

import concourse.bass as bass
import concourse.mybir as mybir
from concourse.alu_op_type import AluOpType as OP
from concourse.bass_utils import run_bass_kernel_spmd

F32 = mybir.dt.float32
BF16 = mybir.dt.bfloat16
I32 = mybir.dt.int32
AF = mybir.ActivationFunctionType
AX = mybir.AxisListType

S, E, H, HD = 4096, 1024, 16, 64
NEC = 8
NTT = 32
NQT = 8
DFF, NDC, NE = 4096, 32, 8
VSLOT = 65
VROW = H * VSLOT
SPAN = 16
CONV_PH1 = 12

ENGINES = ("sync", "tensor", "vector", "scalar", "gpsimd")
QSEM = {"sync": "sq", "gpsimd": "gq", "scalar": "aq"}


class Sched:
    def __init__(self):
        self.ops = {e: [] for e in ENGINES}
        self.cnt = {}

    def op(self, eng, fn, wait=(), inc=None, amt=1):
        w = [(s, v) for (s, v) in wait if v and v > 0]
        incs = []
        if inc is not None:
            self.cnt[inc] = self.cnt.get(inc, 0) + amt
            incs = [(inc, amt)]
        self.ops[eng].append((w, fn, incs))
        return self.cnt[inc] if inc is not None else None

    def dma(self, q, out, in_, wait=()):
        sem = QSEM[q]
        return self.op(q, lambda e, o=out, i=in_: e.dma_start(out=o, in_=i),
                       wait, inc=sem, amt=16)

    def emit(self, block, sems):
        for eng in ENGINES:
            items = self.ops[eng]
            if not items:
                continue

            def body(e, items=items):
                hi = {}
                for w, fn, incs in items:
                    for s, v in w:
                        if hi.get(s, -1) < v:
                            e.wait_ge(sems[s], v)
                            hi[s] = v
                    inst = fn(e)
                    for s, amt in incs:
                        inst.then_inc(sems[s], amt)

            getattr(block, eng)(body)


def build(debug=False, phases=3, n_tt=NTT, n_conv=NE * NDC, dbg_dump=True, n_tt3=NQT, n_exp=NE, m2_on=True, s3parts='wbc'):
    nc = bass.Bass("TRN2", target_bir_lowering=False)
    p = Sched()
    for s in ("sq", "gq", "pe", "dv", "ac", "po", "gp"):
        p.cnt[s] = 0

    ap = lambda t: t.ap()
    x_in = ap(nc.dram_tensor("x", [S, E], F32, kind="ExternalInput"))
    wq_in = ap(nc.dram_tensor("wq", [E, E], F32, kind="ExternalInput"))
    wk_in = ap(nc.dram_tensor("wk", [E, E], F32, kind="ExternalInput"))
    wv_in = ap(nc.dram_tensor("wv", [E, E], F32, kind="ExternalInput"))
    wd_in = ap(nc.dram_tensor("wd", [E, E], F32, kind="ExternalInput"))
    bqkv_in = ap(nc.dram_tensor("bqkv", [3, E], F32, kind="ExternalInput"))
    bd_in = ap(nc.dram_tensor("bd", [1, E], F32, kind="ExternalInput"))
    w1_in = ap(nc.dram_tensor("w1", [NE, E, DFF], F32, kind="ExternalInput"))
    b1_in = ap(nc.dram_tensor("b1", [NE, 32, 128], F32, kind="ExternalInput"))
    w2_in = ap(nc.dram_tensor("w2", [NE, DFF, E], F32, kind="ExternalInput"))
    b2_in = ap(nc.dram_tensor("b2", [1, NE * E], F32, kind="ExternalInput"))
    cossin_in = ap(nc.dram_tensor("cossin", [S, 64], F32, kind="ExternalInput"))
    idbf_in = ap(nc.dram_tensor("idbf", [128, 128], BF16, kind="ExternalInput"))
    idf_in = ap(nc.dram_tensor("idf", [128, 128], F32, kind="ExternalInput"))
    e0b_in = ap(nc.dram_tensor("e0b", [128, 128], BF16, kind="ExternalInput"))
    masks_in = ap(nc.dram_tensor("masks", [128, 4 * 512], BF16, kind="ExternalInput"))
    out_t = ap(nc.dram_tensor("out", [S, E], F32, kind="ExternalOutput"))
    if debug:
        dbg_xnf = ap(nc.dram_tensor("dbg_xnf", [E, S], BF16, kind="ExternalOutput"))
        dbg_q = ap(nc.dram_tensor("dbg_q", [NTT, NEC, 128, 128], BF16, kind="ExternalOutput"))
        dbg_k = ap(nc.dram_tensor("dbg_k", [NTT, NEC, 128, 128], BF16, kind="ExternalOutput"))
        dbg_ctx = ap(nc.dram_tensor("dbg_ctx", [NQT, NEC, 128, 512], BF16, kind="ExternalOutput"))
        dbg_route = ap(nc.dram_tensor("dbg_route", [128, NTT * NE], F32, kind="ExternalOutput"))
        dbg_aux = ap(nc.dram_tensor("dbg_aux", [128, 1], F32, kind="ExternalOutput"))

    xnf_d = ap(nc.dram_tensor("xnf_d", [E, S], BF16))
    qf_d = ap(nc.dram_tensor("qf_d", [NTT, NEC, 128, 128], BF16))
    kf_d = ap(nc.dram_tensor("kf_d", [NTT, NEC, 128, 128], BF16))
    ctx_d = ap(nc.dram_tensor("ctx_d", [NQT, NEC, 128, 512], BF16))
    w1bf_d = ap(nc.dram_tensor("w1bf_d", [NE, NDC, 128, 1024], BF16))
    w2bf_d = ap(nc.dram_tensor("w2bf_d", [NE, NDC, 128, 1024], BF16))

    es = ExitStack()
    with es:
        TG = lambda n, s, d=F32: es.enter_context(nc.sbuf_tensor(n, s, d))

        idbf = TG("idbf_s", [128, 128], BF16)
        idf = TG("idf_s", [128, 128], F32)
        e0b = TG("e0b_s", [128, 128], BF16)
        masks = TG("masks_s", [128, 4 * 512], BF16)
        wroute = TG("wroute", [128, NTT * NE], F32)
        qmax2 = TG("qmax2", [128, H], F32)
        kmax2 = TG("kmax2", [128, H], F32)
        bvecn = TG("bvecn", [128, 1], F32)
        flv = TG("flv", [128, 4], F32)
        fla = TG("fla", [128, 4], F32)
        tailsb = TG("tailsb", [128, 512], F32)

        def flushb(eng, sem="__auto__", wait=()):
            sem = {"vector": "dv", "scalar": "ac"}[eng] if sem == "__auto__" else sem
            p.op(eng, lambda e: e.drain(), wait)
            if eng == "vector":
                return p.op(eng, lambda e: e.tensor_copy(out=flv[:, 0:1],
                                                         in_=flv[:, 1:2]), (), inc=sem)
            return p.op(eng, lambda e: e.copy(out=fla[:, 0:1],
                                              in_=fla[:, 1:2]), (), inc=sem)


        def pe_lag(psum_slice, wait=()):
            p.op("tensor", lambda e, ps=psum_slice: e.matmul(
                out=ps, lhsT=e0b[:], rhs=masks[:, 0:512], start=True, stop=True),
                 wait=wait)
            return p.op("tensor", lambda e, ps=psum_slice: e.matmul(
                out=ps, lhsT=e0b[:], rhs=masks[:, 0:512], start=True, stop=True),
                 (), inc="pe")

        def vfl():
            p.op("vector", lambda e: e.tensor_copy(out=flv[:, 0:1], in_=flv[:, 1:2]))

        def dve_pad(n):
            for _ in range(n):
                p.op("vector", lambda e: e.tensor_copy(out=tailsb[:], in_=masks[:, 0:512]))

        p.dma("sync", idbf[:], idbf_in)
        p.dma("sync", idf[:], idf_in)
        p.dma("sync", e0b[:], e0b_in)
        p.dma("sync", masks[:], masks_in)
        setup_ld = p.cnt["sq"]

        es12 = es.enter_context(ExitStack())
        T12 = lambda n, s, d=F32: es12.enter_context(nc.sbuf_tensor(n, s, d))
        v_sb = T12("v_sb", [128, NTT * VROW], BF16)

        es1 = es12.enter_context(ExitStack())
        T1 = lambda n, s, d=F32: es1.enter_context(nc.sbuf_tensor(n, s, d))
        wq_b = T1("wq_b", [128, NEC * E], BF16)
        wk_b = T1("wk_b", [128, NEC * E], BF16)
        wv_b = T1("wv_b", [128, NEC * E], BF16)
        bqkv_pad = T1("bqkv_pad", [128, 3 * E], BF16)
        xb = [T1(f"xb{i}", [128, E], F32) for i in range(2)]
        cs = [T1(f"cs{i}", [128, 64], F32) for i in range(2)]
        xnf32 = [T1(f"xnf32_{i}", [128, E], F32) for i in range(2)]
        xnbf = [T1(f"xnbf{i}", [128, E], BF16) for i in range(2)]
        xnfsb = [T1(f"xnfsb{i}", [128, E], BF16) for i in range(2)]
        qrot = [T1(f"qrot{i}", [128, E], BF16) for i in range(2)]
        krot = [T1(f"krot{i}", [128, E], BF16) for i in range(2)]
        qfsb = [T1(f"qfsb{i}", [128, E], BF16) for i in range(2)]
        kfsb = [T1(f"kfsb{i}", [128, E], BF16) for i in range(2)]
        r1s = T1("r1s", [128, 512], F32)
        ln_st = [T1(f"lnst{i}", [128, 12], F32) for i in range(2)]
        ln_mv = [T1(f"lnmv{i}", [128, 2], F32) for i in range(2)]
        vareps = [T1(f"vareps{i}", [128, 1], F32) for i in range(2)]
        lnv = [T1(f"lnv{i}", [128, 1], F32) for i in range(2)]
        rstd0 = [T1(f"rstd0_{i}", [128, 1], F32) for i in range(2)]
        rstd = [T1(f"rstd{i}", [128, 1], F32) for i in range(2)]
        nmr = [T1(f"nmr{i}", [128, 1], F32) for i in range(2)]
        ntb = [T1(f"ntb{i}", [128, 1], F32) for i in range(2)]
        epsv = T1("epsv", [128, 1], F32)
        rr = T1("rr", [128, 10], F32)
        sqv = T1("sqv", [128, E], BF16)
        qn2 = T1("qn2", [128, H], F32)
        w1cf = [T1(f"w1cf{i}", [128, 1024], F32) for i in range(2)]
        w1cb = [T1(f"w1cb{i}", [128, 1024], BF16) for i in range(2)]
        w2cf = [T1(f"w2cf{i}", [128, 1024], F32) for i in range(2)]
        w2cb = [T1(f"w2cb{i}", [128, 1024], BF16) for i in range(2)]

        es1p = es1.enter_context(ExitStack())
        pT = [es1p.enter_context(nc.psum_tensor(f"pT{i}", [128, 128], BF16)) for i in range(2)]
        pQ = es1p.enter_context(nc.psum_tensor("pQ", [128, 1024], F32))
        pK = es1p.enter_context(nc.psum_tensor("pK", [128, 1024], F32))
        pV = es1p.enter_context(nc.psum_tensor("pV", [128, 1024], F32))

        p.op("vector", lambda e: e.memset(flv[:], 0.0))
        p.op("vector", lambda e: e.memset(fla[:], 0.0))
        p.op("vector", lambda e: e.memset(wroute[:], 0.0))
        p.op("vector", lambda e: e.memset(bvecn[:], 0.0))
        p.op("vector", lambda e: e.memset(bqkv_pad[:], 0.0))
        p.op("vector", lambda e: e.memset(qmax2[:], 0.0))
        p.op("vector", lambda e: e.memset(kmax2[:], 0.0))
        p.op("vector", lambda e: e.memset(epsv[:], 1e-5))
        ones_ap = v_sb[:].rearrange("p (t h c) -> p t h c", h=H, c=VSLOT)[:, :, :, 64:65]
        p.op("vector", lambda e: e.memset(ones_ap, 1.0))
        pad0 = flushb("vector")

        for i in range(NEC):
            p.dma("gpsimd", wq_b[:, i * E:(i + 1) * E], wq_in[i * 128:(i + 1) * 128, :])
            p.dma("gpsimd", wk_b[:, i * E:(i + 1) * E], wk_in[i * 128:(i + 1) * 128, :])
            p.dma("gpsimd", wv_b[:, i * E:(i + 1) * E], wv_in[i * 128:(i + 1) * 128, :])
        for j in range(3):
            p.dma("gpsimd", bqkv_pad[0:1, j * E:(j + 1) * E], bqkv_in[j:j + 1, :],
                  wait=[("dv", pad0)])
        w_ready = p.cnt["gq"]

        conv_jobs = [(e_, dc) for e_ in range(NE) for dc in range(NDC)][:n_conv]
        conv_i = [0]
        conv_po = {}
        conv_st = {}
        w1bf_done = {}
        w2bf_done = {}

        def emit_conv(njobs):
            for _ in range(njobs):
                i = conv_i[0]
                if i >= len(conv_jobs):
                    return
                conv_i[0] = i + 1
                e_, dc = conv_jobs[i]
                b = i % 2
                w = [("po", conv_po.get(i - 2, 0))]
                for c in range(NEC):
                    p.dma("sync", w1cf[b][:, c * 128:(c + 1) * 128],
                          w1_in[e_, c * 128:(c + 1) * 128, dc * 128:(dc + 1) * 128],
                          wait=w if c == 0 else ())
                p.dma("sync", w2cf[b][:], w2_in[e_, dc * 128:(dc + 1) * 128, :])
                ldc = p.cnt["sq"]
                p.op("gpsimd", lambda e, b=b: e.tensor_copy(out=w1cb[b][:], in_=w1cf[b][:]),
                     wait=[("sq", ldc), ("sq", conv_st.get(i - 2, 0))])
                p.op("gpsimd", lambda e, b=b: e.tensor_copy(out=w2cb[b][:], in_=w2cf[b][:]))
                p.op("gpsimd", lambda e: e.drain())
                conv_po[i] = p.op("gpsimd", lambda e: e.tensor_copy(
                    out=flv[:, 2:3], in_=flv[:, 3:4]), (), inc="po")
                p.dma("sync", w1bf_d[e_, dc], w1cb[b][:], wait=[("po", conv_po[i])])
                p.dma("sync", w2bf_d[e_, dc], w2cb[b][:])
                conv_st[i] = p.cnt["sq"]
                w1bf_done[(e_, dc)] = p.cnt["sq"]
                w2bf_done[(e_, dc)] = p.cnt["sq"]

        pT_free = {0: 0, 1: 0}
        xnf_pending = []
        qk_pending = []
        ck_tile_dv = [0] * (NTT + 4)
        ck_tile_ac = [0] * (NTT + 4)
        ck_xnf_pe = [0] * (NTT + 4)
        ck_qkst_sq = [0] * (NTT + 4)
        ck_rope_dv = [0] * (NTT + 4)

        for tt in range(n_tt):
            b = tt % 2
            lw = []
            if tt >= 2:
                lw = [("dv", ck_tile_dv[tt - 2]), ("ac", ck_tile_ac[tt - 2])]
            p.dma("sync", xb[b][:], x_in[tt * 128:(tt + 1) * 128, :], wait=lw)
            p.dma("sync", cs[b][:], cossin_in[tt * 128:(tt + 1) * 128, :])
            ld_x = p.cnt["sq"]

            p.op("vector", lambda e, b=b: e.bn_stats(out=ln_st[b][:, 0:6],
                                                     in_=xb[b][:, 0:512]),
                 wait=[("sq", ld_x)])
            p.op("vector", lambda e, b=b: e.bn_stats(out=ln_st[b][:, 6:12],
                                                     in_=xb[b][:, 512:1024]))
            vfl()
            p.op("vector", lambda e, b=b: e.bn_aggr(out=ln_mv[b][:], in_=ln_st[b][:]))
            vfl()
            p.op("vector", lambda e, b=b: e.tensor_scalar(
                out=vareps[b][:], in0=ln_mv[b][:, 1:2], scalar1=epsv[:], scalar2=None,
                op0=OP.add))
            vfl()

            p.op("vector", lambda e, b=b: e.tensor_scalar(
                out=rstd0[b][:].bitcast(I32), in0=vareps[b][:].bitcast(I32),
                scalar1=1, scalar2=None, op0=OP.arith_shift_right))
            vfl()
            p.op("vector", lambda e, b=b: e.tensor_scalar(
                out=rstd0[b][:].bitcast(I32), in0=rstd0[b][:].bitcast(I32),
                scalar1=0x5f3759df, scalar2=-1, op0=OP.subtract, op1=OP.mult))
            vfl()
            for it in range(3):
                dst = rstd[b] if it == 2 else rstd0[b]
                p.op("vector", lambda e, b=b: e.tensor_tensor(
                    out=ntb[b][:], in0=rstd0[b][:], in1=rstd0[b][:], op=OP.mult))
                vfl()
                p.op("vector", lambda e, b=b: e.tensor_tensor(
                    out=ntb[b][:], in0=ntb[b][:], in1=vareps[b][:], op=OP.mult))
                vfl()
                p.op("vector", lambda e, b=b: e.tensor_scalar(
                    out=ntb[b][:], in0=ntb[b][:], scalar1=-0.5, scalar2=1.5,
                    op0=OP.mult, op1=OP.add))
                vfl()
                p.op("vector", lambda e, b=b, dst=dst: e.tensor_tensor(
                    out=dst[:], in0=rstd0[b][:], in1=ntb[b][:], op=OP.mult))
                vfl()
            p.op("vector", lambda e, b=b: e.tensor_scalar(
                out=xnf32[b][:], in0=xb[b][:], scalar1=ln_mv[b][:, 0:1], scalar2=None,
                op0=OP.subtract))
            vfl()
            p.op("vector", lambda e, b=b: e.tensor_scalar(
                out=xnbf[b][:], in0=xnf32[b][:], scalar1=rstd[b][:], scalar2=None,
                op0=OP.mult))
            ck_tile_ac[tt] = p.cnt["ac"]
            dv_xnbf = flushb("vector")

            p.op("vector", lambda e, b=b: e.tensor_tensor(
                out=rr[:, 0:1], in0=xb[b][:, 0:1], in1=xb[b][:, 1:2], op=OP.add))
            vfl()
            p.op("vector", lambda e, b=b: e.scalar_tensor_tensor(
                out=rr[:, 0:1], in0=ln_mv[b][:, 0:1], scalar=-2.0, in1=rr[:, 0:1],
                op0=OP.mult, op1=OP.add))
            vfl()
            p.op("vector", lambda e, b=b: e.tensor_tensor(
                out=rr[:, 0:1], in0=rr[:, 0:1], in1=rstd[b][:], op=OP.mult))
            vfl()
            p.op("vector", lambda e: e.tensor_scalar(
                out=rr[:, 1:2], in0=rr[:, 0:1], scalar1=-1.0, scalar2=None, op0=OP.mult))
            vfl()
            p.op("vector", lambda e: e.tensor_tensor(
                out=rr[:, 1:2], in0=rr[:, 0:1], in1=rr[:, 1:2], op=OP.max))
            vfl()
            p.op("vector", lambda e: e.tensor_scalar(
                out=rr[:, 2:3], in0=rr[:, 0:1], scalar1=0.0, scalar2=None, op0=OP.is_lt))
            p.op("vector", lambda e: e.memset(rr[:, 4:5], 0.0))
            vfl()
            for th in (32.0, 16.0, 8.0, 4.0, 2.0, 1.0):
                p.op("vector", lambda e, th=th: e.tensor_scalar(
                    out=rr[:, 3:4], in0=rr[:, 1:2], scalar1=th, scalar2=None, op0=OP.is_ge))
                vfl()
                p.op("vector", lambda e, th=th: e.scalar_tensor_tensor(
                    out=rr[:, 1:2], in0=rr[:, 3:4], scalar=-th, in1=rr[:, 1:2],
                    op0=OP.mult, op1=OP.add))
                vfl()
                if th <= 4.0:
                    p.op("vector", lambda e, th=th: e.scalar_tensor_tensor(
                        out=rr[:, 4:5], in0=rr[:, 3:4], scalar=th, in1=rr[:, 4:5],
                        op0=OP.mult, op1=OP.add))
                    vfl()
            p.op("vector", lambda e: e.tensor_scalar(
                out=rr[:, 5:6], in0=rr[:, 4:5], scalar1=-1.0, scalar2=8.0,
                op0=OP.mult, op1=OP.add))
            vfl()
            p.op("vector", lambda e: e.tensor_scalar(
                out=rr[:, 3:4], in0=rr[:, 4:5], scalar1=0.0, scalar2=None, op0=OP.is_le))
            vfl()
            p.op("vector", lambda e: e.scalar_tensor_tensor(
                out=rr[:, 5:6], in0=rr[:, 3:4], scalar=-8.0, in1=rr[:, 5:6],
                op0=OP.mult, op1=OP.add))
            vfl()
            p.op("vector", lambda e: e.tensor_tensor(
                out=rr[:, 7:8], in0=rr[:, 5:6], in1=rr[:, 4:5], op=OP.subtract))
            vfl()
            p.op("vector", lambda e: e.scalar_tensor_tensor(
                out=rr[:, 6:7], in0=rr[:, 7:8], scalar=rr[:, 2:3], in1=rr[:, 4:5],
                op0=OP.mult, op1=OP.add))
            vfl()
            for e_ in range(NE):
                p.op("vector", lambda e, e_=e_: e.tensor_scalar(
                    out=rr[:, 8:9], in0=rr[:, 6:7], scalar1=float(e_), scalar2=None,
                    op0=OP.is_equal))
                p.op("vector", lambda e, e_=e_: e.tensor_scalar(
                    out=rr[:, 9:10], in0=rr[:, 6:7], scalar1=float((e_ - 1) % NE),
                    scalar2=None, op0=OP.is_equal))
                vfl()
                p.op("vector", lambda e, e_=e_, tt=tt: e.tensor_tensor(
                    out=wroute[:, tt * NE + e_:tt * NE + e_ + 1],
                    in0=rr[:, 8:9], in1=rr[:, 9:10], op=OP.add))
                vfl()
            dve_pad(2)

            for ec in range(NEC):
                tb = ec % 2
                w = [("dv", dv_xnbf), ("dv", pT_free[tb]), ("sq", setup_ld)]
                if tt >= 2:
                    w += [("pe", ck_xnf_pe[tt - 2]), ("sq", ck_qkst_sq[tt - 2])]
                pe_tr = p.op("tensor", lambda e, b=b, ec=ec, tb=tb: e.transpose(
                    out=pT[tb][:], in_=xnbf[b][:, ec * 128:(ec + 1) * 128],
                    identity=idbf[:]), wait=w, inc="pe")
                pT_free[tb] = p.op("vector", lambda e, b=b, ec=ec, tb=tb: e.tensor_copy(
                    out=xnfsb[b][:, ec * 128:(ec + 1) * 128], in_=pT[tb][:]),
                     wait=[("pe", pe_tr)], inc="dv")
            dv_xnf = flushb("vector")
            xnf_pending.append((tt, b, dv_xnf))

            wmm = [("gq", w_ready), ("dv", dv_xnf)]
            if xnf_pending and xnf_pending[0][0] < tt:
                t0, b0, g0 = xnf_pending.pop(0)
                for ec in range(NEC):
                    p.dma("sync", xnf_d[ec * 128:(ec + 1) * 128, t0 * 128:(t0 + 1) * 128],
                          xnfsb[b0][:, ec * 128:(ec + 1) * 128],
                          wait=[("dv", ck_rope_dv[t0])])
            if qk_pending and qk_pending[0][0] < tt:
                t0, b0, g0 = qk_pending.pop(0)
                for (fdst, hbm) in ((qfsb, qf_d), (kfsb, kf_d)):
                    for ec in range(NEC):
                        p.dma("sync", hbm[t0, ec], fdst[b0][:, ec * 128:(ec + 1) * 128],
                              wait=[("dv", p.cnt["dv"])])
                ck_qkst_sq[t0] = p.cnt["sq"]
            if tt >= 1:
                wmm.append(("dv", ck_rope_dv[tt - 1]))
            first = True
            for (jb, wt, pdst) in ((0, wq_b, pQ), (1, wk_b, pK), (2, wv_b, pV)):
                for eo in range(2):
                    for ein in range(NEC):
                        p.op("tensor", lambda e, b=b, wt=wt, pdst=pdst, eo=eo, ein=ein: e.matmul(
                            out=pdst[:, eo * 512:(eo + 1) * 512],
                            lhsT=xnfsb[b][:, ein * 128:(ein + 1) * 128],
                            rhs=wt[:, ein * E + eo * 512:ein * E + eo * 512 + 512],
                            start=(ein == 0), stop=False),
                             wait=wmm if first else ())
                        first = False
                    p.op("tensor", lambda e, jb=jb, pdst=pdst, eo=eo: e.matmul(
                        out=pdst[:, eo * 512:(eo + 1) * 512], lhsT=e0b[:],
                        rhs=bqkv_pad[:, jb * E + eo * 512:jb * E + eo * 512 + 512],
                        start=False, stop=True), inc="pe")
            pe_qkv = p.cnt["pe"]

            for (pdst, dst) in ((pQ, qrot), (pK, krot)):
                for eo in range(2):
                    ps3 = pdst[:, eo * 512:(eo + 1) * 512].rearrange("p (h d) -> p h d", d=64)
                    d3 = dst[b][:, eo * 512:(eo + 1) * 512].rearrange("p (h d) -> p h d", d=64)
                    r13 = r1s[:, 0:256].rearrange("p (h d) -> p h d", d=32)
                    cb = cs[b][:, 0:32].unsqueeze(1).broadcast_to([128, 8, 32])
                    sa = cs[b][:, 32:48].unsqueeze(1).broadcast_to([128, 8, 16])
                    sb2 = cs[b][:, 48:64].unsqueeze(1).broadcast_to([128, 8, 16])
                    p.op("vector", lambda e, ps3=ps3, r13=r13, cb=cb: e.tensor_tensor(
                        out=r13, in0=ps3[:, :, 0:32], in1=cb, op=OP.mult),
                         wait=[("pe", pe_qkv), ("sq", ld_x)])
                    p.op("vector", lambda e, ps3=ps3, d3=d3, sa=sa: e.tensor_tensor(
                        out=d3[:, :, 0:16], in0=ps3[:, :, 16:32], in1=sa, op=OP.mult))
                    p.op("vector", lambda e, ps3=ps3, d3=d3, sb2=sb2: e.tensor_tensor(
                        out=d3[:, :, 16:32], in0=ps3[:, :, 0:16], in1=sb2, op=OP.mult))
                    vfl()
                    p.op("vector", lambda e, d3=d3, r13=r13: e.tensor_tensor(
                        out=d3[:, :, 0:32], in0=r13, in1=d3[:, :, 0:32], op=OP.subtract))
                    p.op("vector", lambda e, ps3=ps3, d3=d3: e.tensor_copy(
                        out=d3[:, :, 32:64], in_=ps3[:, :, 32:64]))
            for eo in range(2):
                vdst = v_sb[:].rearrange("p (t h c) -> p t h c", h=H, c=VSLOT)[
                    :, tt, eo * 8:(eo + 1) * 8, 0:64]
                p.op("vector", lambda e, eo=eo, vdst=vdst: e.tensor_copy(
                    out=vdst,
                    in_=pV[:, eo * 512:(eo + 1) * 512].rearrange("p (h d) -> p h d", d=64)))
            ck_rope_dv[tt] = flushb("vector")

            p.op("vector", lambda e, b=b: e.tensor_tensor(
                out=sqv[:], in0=qrot[b][:], in1=qrot[b][:], op=OP.mult))
            vfl()
            p.op("vector", lambda e: e.tensor_reduce(
                out=qn2[:].unsqueeze(2), in_=sqv[:].rearrange("p (h d) -> p h d", d=64),
                axis=AX.X, op=OP.add))
            vfl()
            p.op("vector", lambda e: e.tensor_tensor(
                out=qmax2[:], in0=qmax2[:], in1=qn2[:], op=OP.max))
            p.op("vector", lambda e, b=b: e.tensor_tensor(
                out=sqv[:], in0=krot[b][:], in1=krot[b][:], op=OP.mult))
            vfl()
            p.op("vector", lambda e: e.tensor_reduce(
                out=qn2[:].unsqueeze(2), in_=sqv[:].rearrange("p (h d) -> p h d", d=64),
                axis=AX.X, op=OP.add))
            vfl()
            p.op("vector", lambda e: e.tensor_tensor(
                out=kmax2[:], in0=kmax2[:], in1=qn2[:], op=OP.max))

            for (src, fdst) in ((qrot, qfsb), (krot, kfsb)):
                for ec in range(NEC):
                    tb = ec % 2
                    pe_tr = p.op("tensor", lambda e, b=b, src=src, ec=ec, tb=tb: e.transpose(
                        out=pT[tb][:], in_=src[b][:, ec * 128:(ec + 1) * 128],
                        identity=idbf[:]),
                         wait=[("dv", ck_rope_dv[tt]), ("dv", pT_free[tb])], inc="pe")
                    pT_free[tb] = p.op("vector", lambda e, b=b, fdst=fdst, ec=ec, tb=tb:
                                       e.tensor_copy(out=fdst[b][:, ec * 128:(ec + 1) * 128],
                                                     in_=pT[tb][:]),
                                       wait=[("pe", pe_tr)], inc="dv")
            dv_qkf = flushb("vector")
            qk_pending.append((tt, b, dv_qkf))
            ck_xnf_pe[tt] = pe_qkv
            ck_tile_dv[tt] = dv_qkf

            emit_conv(CONV_PH1)

        dve_pad(8)
        flushb("vector")
        for (t0, b0, g0) in xnf_pending:
            for ec in range(NEC):
                p.dma("sync", xnf_d[ec * 128:(ec + 1) * 128, t0 * 128:(t0 + 1) * 128],
                      xnfsb[b0][:, ec * 128:(ec + 1) * 128], wait=[("dv", p.cnt["dv"])])
        xnf_pending = []
        for (t0, b0, g0) in qk_pending:
            for (fdst, hbm) in ((qfsb, qf_d), (kfsb, kf_d)):
                for ec in range(NEC):
                    p.dma("sync", hbm[t0, ec], fdst[b0][:, ec * 128:(ec + 1) * 128],
                          wait=[("dv", p.cnt["dv"])])
            ck_qkst_sq[t0] = p.cnt["sq"]
        qk_pending = []
        ph1_sq = p.cnt["sq"]
        ph1_dv = p.cnt["dv"]
        ph1_ac = p.cnt["ac"]
        ph1_pe = p.cnt["pe"]

        es1p.close()
        es1.close()

        es2 = es12.enter_context(ExitStack())
        T2 = lambda n, s, d=F32: es2.enter_context(nc.sbuf_tensor(n, s, d))
        k_sb = [T2(f"ksb{ec}", [128, S], BF16) for ec in range(NEC)]
        q_sb = T2("q_sb", [128, NEC * 512], BF16)
        probs = T2("probs", [128, SPAN * 1024], BF16)
        dnA = T2("dnA", [1, 512], F32)
        dnB = T2("dnB", [1, 512], F32)
        rcpf = T2("rcpf", [1, 1024], F32)
        rpad = T2("rpad", [128, 1024], BF16)
        rbc_sb = T2("rbc_sb", [128, 1024], F32)
        ctxp = [T2(f"ctxp{i}", [128, 512], BF16) for i in range(2)]
        bred = T2("bred", [128, 1], F32)
        bredT = T2("bredT", [1, 128], F32)
        bsc = T2("bsc", [1, 2], F32)
        bnp = T2("bnp", [128, 1], BF16)

        es2p = es2.enter_context(ExitStack())
        psAB = [es2p.enter_context(nc.psum_tensor(f"psAB{i}", [128, 1024], F32))
                for i in range(2)]
        ctxA = es2p.enter_context(nc.psum_tensor("ctxA", [128, 512], F32))
        ctxB = es2p.enter_context(nc.psum_tensor("ctxB", [128, 512], F32))
        prbc = es2p.enter_context(nc.psum_tensor("prbc", [128, 1024], F32))

        if phases >= 2:
            p.op("vector", lambda e: e.memset(rpad[:], 0.0), wait=[("pe", ph1_pe)])
            p.op("vector", lambda e: e.memset(bnp[:], 0.0))
            p.op("vector", lambda e: e.tensor_tensor(out=qmax2[:], in0=qmax2[:], in1=kmax2[:],
                                                     op=OP.max), wait=[("dv", ph1_dv)])
            vfl()
            vfl()
            p.op("vector", lambda e: e.tensor_reduce(out=bred[:], in_=qmax2[:], axis=AX.X,
                                                     op=OP.max))
            dv_b1_ = flushb("vector")
            pe_b1_ = p.op("tensor", lambda e: e.transpose(out=prbc[0:1, 0:128], in_=bred[:],
                                                          identity=idf[:]),
                          wait=[("dv", dv_b1_)], inc="pe")
            p.op("vector", lambda e: e.tensor_copy(out=bredT[:], in_=prbc[0:1, 0:128]),
                 wait=[("pe", pe_b1_)])
            vfl()
            p.op("vector", lambda e: e.tensor_reduce(out=bsc[0:1, 0:1], in_=bredT[:].unsqueeze(1),
                                                     axis=AX.X, op=OP.max))
            vfl()
            vfl()
            p.op("vector", lambda e: e.tensor_scalar(out=bnp[0:1, 0:1], in0=bsc[0:1, 0:1],
                                                     scalar1=-0.125, scalar2=None, op0=OP.mult))
            dv_b2_ = flushb("vector")
            pe_b2_ = p.op("tensor", lambda e: e.matmul(out=prbc[:, 0:1], lhsT=e0b[:], rhs=bnp[:],
                                                       start=True, stop=True),
                          wait=[("dv", dv_b2_)], inc="pe")
            p.op("vector", lambda e: e.tensor_copy(out=bvecn[:], in_=prbc[:, 0:1]),
                 wait=[("pe", pe_b2_)])
            dv_bvec = flushb("vector")

        dv_bvec = locals().get('dv_bvec', 0) if phases >= 2 else 0
        ck_q_pe = 0
        ck_rbc_dv = 0
        ctx_pending = []
        ctx_store_sq = {}
        ck_beta_pe = 0
        ck_ctx_dv = 0
        psAB_free = {0: 0, 1: 0}

        for qt in (range(NQT) if phases >= 2 else []):
            nkt = (qt + 1) * 4
            wq_ = [("pe", ck_q_pe), ("sq", ph1_sq), ("dv", dv_bvec)]
            for ec in range(NEC):
                for j in range(4):
                    p.dma("sync", q_sb[:, ec * 512 + j * 128:ec * 512 + (j + 1) * 128],
                          qf_d[qt * 4 + j, ec], wait=wq_ if (ec == 0 and j == 0) else ())
            for kt in range(qt * 4, qt * 4 + 4):
                for ec in range(NEC):
                    p.dma("sync", k_sb[ec][:, kt * 128:(kt + 1) * 128], kf_d[kt, ec])
            ld_qk = p.cnt["sq"]

            for hp in range(NEC):
                spans = [(s0, min(s0 + SPAN, nkt)) for s0 in range(0, nkt, SPAN)]
                first_ctx = True
                for (s0, s1) in spans:
                    for kt in range(s0, s1):
                        par = kt % 2
                        w = [("sq", ld_qk), ("ac", psAB_free[par]), ("pe", ck_beta_pe)]
                        p.op("tensor", lambda e, hp=hp, kt=kt, par=par: e.matmul(
                            out=psAB[par][:, 0:512],
                            lhsT=k_sb[hp][0:64, kt * 128:(kt + 1) * 128],
                            rhs=q_sb[0:64, hp * 512:(hp + 1) * 512],
                            start=True, stop=True, tile_position=(0, 0)), wait=w)
                        pe_sc = p.op("tensor", lambda e, hp=hp, kt=kt, par=par: e.matmul(
                            out=psAB[par][:, 512:1024],
                            lhsT=k_sb[hp][64:128, kt * 128:(kt + 1) * 128],
                            rhs=q_sb[64:128, hp * 512:(hp + 1) * 512],
                            start=True, stop=True, tile_position=(64, 0)), inc="pe")
                        if kt == s1 - 1:
                            pe_sc = pe_lag(prbc[:, 0:512], wait=[("dv", ck_rbc_dv)])
                        psAB_free[par] = p.op("scalar", lambda e, par=par, sl=kt - s0:
                                              e.activation(
                                                  out=probs[:, sl * 1024:(sl + 1) * 1024],
                                                  in_=psAB[par][:], func=AF.Exp,
                                                  scale=0.125, bias=bvecn[:]),
                                              wait=[("pe", pe_sc)], inc="ac")
                    ac_alpha = flushb("scalar")
                    dv_alpha = 0
                    for kt in range(max(s0, qt * 4), s1):
                        if kt < qt * 4:
                            continue
                        v_ = kt - qt * 4
                        sl = kt - s0
                        p.op("vector", lambda e, sl=sl, v_=v_: e.tensor_tensor(
                            out=probs[:, sl * 1024:(sl + 1) * 1024].rearrange(
                                "p (h q) -> p h q", q=512),
                            in0=probs[:, sl * 1024:(sl + 1) * 1024].rearrange(
                                "p (h q) -> p h q", q=512),
                            in1=masks[:, v_ * 512:(v_ + 1) * 512].unsqueeze(1)
                                .broadcast_to([128, 2, 512]),
                            op=OP.mult), wait=[("ac", ac_alpha)])
                    dv_alpha = flushb("vector")
                    for kt in range(s0, s1):
                        sl = kt - s0
                        w = [("ac", ac_alpha), ("dv", dv_alpha)]
                        if first_ctx and kt == s0:
                            w.append(("dv", ck_ctx_dv))
                        vbase = kt * VROW
                        vsA = v_sb[:, vbase + (2 * hp) * VSLOT:vbase + (2 * hp) * VSLOT + VSLOT]
                        vsB = v_sb[:, vbase + (2 * hp + 1) * VSLOT:
                                   vbase + (2 * hp + 1) * VSLOT + VSLOT]
                        last = (kt == nkt - 1)
                        p.op("tensor", lambda e, vsA=vsA, sl=sl, st0=(first_ctx and kt == s0),
                             last=last: e.matmul(
                                 out=ctxA[0:65, :], lhsT=vsA,
                                 rhs=probs[:, sl * 1024:sl * 1024 + 512],
                                 start=st0, stop=last), wait=w)
                        p.op("tensor", lambda e, vsB=vsB, sl=sl, st0=(first_ctx and kt == s0),
                             last=last: e.matmul(
                                 out=ctxB[0:65, :], lhsT=vsB,
                                 rhs=probs[:, sl * 1024 + 512:sl * 1024 + 1024],
                                 start=st0, stop=last), inc="pe")
                    first_ctx = False
                    ck_beta_pe = p.cnt["pe"]
                pe_beta = pe_lag(prbc[:, 0:512])
                p.op("vector", lambda e: e.tensor_copy(out=dnA[:], in_=ctxA[64:65, :]),
                     wait=[("pe", pe_beta)])
                p.op("vector", lambda e: e.tensor_copy(out=dnB[:], in_=ctxB[64:65, :]))
                vfl()
                p.op("vector", lambda e: e.reciprocal(out=rcpf[0:1, 0:512], in_=dnA[:]))
                p.op("vector", lambda e: e.reciprocal(out=rcpf[0:1, 512:1024], in_=dnB[:]))
                vfl()
                p.op("vector", lambda e: e.tensor_copy(out=rpad[0:1, :], in_=rcpf[:]))
                dv_rp = flushb("vector")
                p.op("tensor", lambda e: e.matmul(out=prbc[:, 0:512], lhsT=e0b[:],
                                                  rhs=rpad[:, 0:512], start=True, stop=True),
                     wait=[("dv", dv_rp)])
                p.op("tensor", lambda e: e.matmul(
                    out=prbc[:, 512:1024], lhsT=e0b[:], rhs=rpad[:, 512:1024],
                    start=True, stop=True))
                pe_rbc = pe_lag(psAB[0][:, 0:512])
                cb = hp % 2
                wctx = [("pe", pe_rbc)]
                if ctx_pending:
                    wctx.append(("sq", ctx_store_sq.get(cb, 0)))
                ck_rbc_dv = p.op("vector", lambda e: e.tensor_copy(out=rbc_sb[:], in_=prbc[:]),
                     wait=wctx, inc="dv")
                vfl()
                p.op("vector", lambda e, cb=cb: e.tensor_tensor(
                    out=ctxp[cb][0:64, :], in0=ctxA[0:64, :], in1=rbc_sb[0:64, 0:512],
                    op=OP.mult))
                p.op("vector", lambda e, cb=cb: e.tensor_tensor(
                    out=ctxp[cb][64:128, :], in0=ctxB[0:64, :], in1=rbc_sb[64:128, 512:1024],
                    op=OP.mult))
                ck_ctx_dv = flushb("vector")
                if ctx_pending:
                    (q0, h0, c0) = ctx_pending.pop(0)
                    p.dma("sync", ctx_d[q0, h0], ctxp[c0][:], wait=[("dv", p.cnt["dv"])])
                    ctx_store_sq[c0] = p.cnt["sq"]
                ctx_pending.append((qt, hp, cb))
            ck_q_pe = p.cnt["pe"]
            emit_conv(24)

        if phases >= 2:
            dve_pad(8)
            flushb("vector")
            for (q0, h0, c0) in ctx_pending:
                p.dma("sync", ctx_d[q0, h0], ctxp[c0][:], wait=[("dv", p.cnt["dv"])])
            ctx_pending = []
        emit_conv(10 ** 9)
        ph2_sq = p.cnt["sq"]
        ph2_dv = p.cnt["dv"]
        ph2_ac = p.cnt["ac"]
        ph2_pe = p.cnt["pe"]
        ph2_gq = p.cnt["gq"]
        ph2_po = p.cnt["po"]

        es2p.close()
        es2.close()
        es12.close()

        es3 = es.enter_context(ExitStack())
        T3 = lambda n, s, d=F32: es3.enter_context(nc.sbuf_tensor(n, s, d))
        h_sb = T3("h_sb", [128, NDC * 512], BF16)
        acc = [T3(f"acc{i}", [128, 4 * E], F32) for i in range(2)]
        xn_sb0 = T3("xn_sb0", [128, NEC * 512], BF16)
        ctx_sb0 = T3("ctx_sb0", [128, NEC * 512], BF16)
        xn_sb = [xn_sb0, xn_sb0]
        ctx_sb = [ctx_sb0, ctx_sb0]
        x_sb = [T3(f"x_sb{i}", [128, E], F32) for i in range(4)]
        wd_b = T3("wd_b", [128, NEC * E], BF16)
        w1s = [T3(f"w1s{i}", [128, 1024], BF16) for i in range(2)]
        w2s = [T3(f"w2s{i}", [128, 1024], BF16) for i in range(2)]
        b1_sb = T3("b1_sb", [128, NE * 32], F32)
        b1st = T3("b1st", [32, 128], F32)
        bdbc = T3("bdbc", [128, E], F32)
        bd_pad = T3("bd_pad", [128, E], F32)
        b2_pad = T3("b2_pad", [128, NE * E], BF16)
        b2st = T3("b2st", [1, NE * E], F32)
        e0f = T3("e0f", [128, 128], F32)
        evw = [T3(f"evw{i}", [128, 512], F32) for i in range(8)]
        evy = evw

        es3p = es3.enter_context(ExitStack())
        py = es3p.enter_context(nc.psum_tensor("py", [128, 4096], F32))

        if phases >= 3:
            ph3_w = [("pe", ph2_pe), ("dv", ph2_dv), ("ac", ph2_ac), ("sq", ph2_sq),
                     ("po", ph2_po)]

            if "w" in s3parts:
                for i in range(NEC):
                    p.dma("gpsimd", wd_b[:, i * E:(i + 1) * E], wd_in[i * 128:(i + 1) * 128, :],
                          wait=ph3_w if i == 0 else ())
            wd_ready = p.cnt["gq"]

            p.op("vector", lambda e: e.memset(b2_pad[:], 0.0), wait=ph3_w)
            p.op("vector", lambda e: e.memset(bd_pad[:], 0.0))
            p.op("vector", lambda e: e.memset(e0f[:], 0.0))
            p.op("vector", lambda e: e.memset(e0f[0:1, :], 1.0))
            dv_z = flushb("vector")
            p.dma("sync", b2st[:], b2_in, wait=ph3_w)
            ld_b2 = p.cnt["sq"]
            p.op("vector", lambda e: e.tensor_scalar(
                out=b2_pad[0:1, :], in0=b2st[:], scalar1=0.5, scalar2=None, op0=OP.mult),
                 wait=[("sq", ld_b2)])
            p.dma("sync", bd_pad[0:1, :], bd_in, wait=[("dv", dv_z)])
            ld_bd = p.cnt["sq"]
            dv_pads = flushb("vector")
            if "b" in s3parts:
                pe_bd = None
                for eo in range(2):
                    pe_bd = p.op("tensor", lambda e, eo=eo: e.matmul(
                        out=py[:, eo * 512:(eo + 1) * 512], lhsT=e0f[:],
                        rhs=bd_pad[:, eo * 512:(eo + 1) * 512], start=True, stop=True),
                         wait=([("sq", ld_bd), ("dv", dv_pads)] + ph3_w) if eo == 0 else (),
                         inc="pe" if eo == 1 else None)
                pe_bd = pe_lag(py[:, 1024:1536])
                p.op("scalar", lambda e: e.activation(out=bdbc[:], in_=py[:, 0:1024],
                                                      func=AF.Identity, scale=1.0),
                     wait=[("pe", pe_bd)])
            else:
                p.op("vector", lambda e: e.memset(bdbc[:], 0.0), wait=ph3_w)
            dv_bdbc = flushb("vector")
            ac_bdbc = flushb("scalar")

            b1_free = {}
            if "c" in s3parts:
                for e_ in range(NE):
                    p.dma("sync", b1st[:], b1_in[e_],
                          wait=[("pe", b1_free.get(e_ - 1, 0))])
                    ldb1 = p.cnt["sq"]
                    b1_free[e_] = p.op("tensor", lambda e, e_=e_: e.transpose(
                        out=py[:, 3584 + e_ * 32:3616 + e_ * 32], in_=b1st[:],
                        identity=idf[0:32, 0:32]),
                        wait=[("sq", ldb1), ("ac", ac_bdbc)], inc="pe")
                pe_b1all = pe_lag(py[:, 3072:3584])
                for e_ in range(NE):
                    p.op("vector", lambda e, e_=e_: e.tensor_scalar(
                        out=b1_sb[:, e_ * 32:(e_ + 1) * 32],
                        in0=py[:, 3584 + e_ * 32:3616 + e_ * 32], scalar1=0.5,
                        scalar2=None, op0=OP.mult), wait=[("pe", pe_b1all)], inc="dv")
            dv_b1sb = flushb("vector")

            dve_pad(8)
            dv_b1sb2 = flushb("vector")
        else:
            wd_ready = 0; dv_bdbc = 0; dv_b1sb = 0; ac_bdbc = 0; dv_b1sb2 = 0
        ck_acc_gq = [0, 0]
        ck_tt_pe = [0] * (NQT + 4)
        ck_init_gp = [0] * (NQT + 4)
        bank_free = {}
        evw_free = [0] * 8
        evy_free = evw_free
        m1_bank_free = {}
        slab_pe = {}

        for tt in (range(n_tt3) if phases >= 3 else []):
            b = tt % 2
            wld = [("sq", ph2_sq)]
            if tt >= 1:
                wld.append(("pe", ck_tt_pe[tt - 1]))
            for ec in range(NEC):
                p.dma("sync", xn_sb[b][:, ec * 512:(ec + 1) * 512],
                      xnf_d[ec * 128:(ec + 1) * 128, tt * 512:(tt + 1) * 512],
                      wait=wld if ec == 0 else ())
            for hp in range(NEC):
                p.dma("sync", ctx_sb[b][:, hp * 512:(hp + 1) * 512], ctx_d[tt, hp])
            ld_xc = p.cnt["sq"]

            for sub in range(4):
                p.dma("sync", x_sb[sub][:],
                      x_in[tt * 512 + sub * 128:tt * 512 + (sub + 1) * 128, :],
                      wait=[("gp", ck_init_gp[tt - 1] if tt >= 1 else 0)])
                ld_xs = p.cnt["sq"]
                wacc = [("sq", ld_xs), ("ac", ac_bdbc)]
                if tt >= 2:
                    wacc.append(("gq", ck_acc_gq[b]))
                p.op("gpsimd", lambda e, b=b, sub=sub: e.tensor_tensor(
                    out=acc[b][:, sub * E:(sub + 1) * E], in0=x_sb[sub][:],
                    in1=bdbc[:], op=OP.add), wait=wacc, inc="gp")
            ck_init_gp[tt] = p.cnt["gp"]
            pe_wd = None
            for sub in range(4):
                for eo in range(2):
                    bank = sub * 2 + eo
                    w = [("sq", ld_xc), ("gq", wd_ready), ("dv", dv_b1sb2),
                         ("ac", bank_free.get(bank, 0))]
                    for hp in range(NEC):
                        pe_wd = p.op("tensor", lambda e, b=b, hp=hp, sub=sub, eo=eo, bank=bank:
                             e.matmul(
                                 out=py[:, bank * 512:(bank + 1) * 512],
                                 lhsT=ctx_sb[b][:, hp * 512 + sub * 128:
                                                hp * 512 + (sub + 1) * 128],
                                 rhs=wd_b[:, hp * E + eo * 512:hp * E + eo * 512 + 512],
                                 start=(hp == 0), stop=(hp == NEC - 1)),
                             wait=w if hp == 0 else (),
                             inc="pe" if hp == NEC - 1 else None)
            pe_gate = pe_wd
            ev_flush = []
            for sub in range(4):
                for eo in range(2):
                    bank = sub * 2 + eo
                    j = bank
                    bank_free[bank] = p.op("scalar", lambda e, bank=bank, j=j: e.activation(
                        out=evw[j][:], in_=py[:, bank * 512:(bank + 1) * 512],
                        func=AF.Identity, scale=1.0),
                        wait=[("pe", pe_gate), ("gp", evw_free[j])], inc="ac")
                    if bank == 0:
                        pe_gate = pe_lag(py[:, 0:512], wait=[("ac", bank_free[0])])
                    ev_flush.append(flushb("scalar"))
            for _ in range(2):
                p.op("scalar", lambda e: e.activation(out=tailsb[:], in_=masks[:, 0:512],
                                                      func=AF.Identity, scale=1.0))
                ev_flush.append(flushb("scalar"))
            for sub in range(4):
                for eo in range(2):
                    bank = sub * 2 + eo
                    j = bank
                    p.op("gpsimd", lambda e, b=b, sub=sub, eo=eo, j=j:
                        e.tensor_tensor(
                            out=acc[b][:, sub * E + eo * 512:sub * E + eo * 512 + 512],
                            in0=evw[j][:],
                            in1=acc[b][:, sub * E + eo * 512:sub * E + eo * 512 + 512],
                            op=OP.add), wait=[("ac", ev_flush[bank + 2])], inc="gp")
            p.op("gpsimd", lambda e: e.tensor_copy(out=tailsb[:], in_=bdbc[:, 0:512]),
                 inc="gp")
            gfree = p.cnt["gp"]
            for j in range(8):
                evw_free[j] = gfree

            for e_ in range(n_exp):
                pe_m1_of = {}
                for dc in range(NDC):
                    sb_ = dc % 2
                    p.dma("sync", w1s[sb_][:], w1bf_d[e_, dc],
                          wait=[("sq", w1bf_done.get((e_, dc), 0)),
                                ("pe", slab_pe.get(("w1", sb_), 0))])
                    ld_s = p.cnt["sq"]
                    bank = 2 + (dc % 3)
                    w = [("sq", ld_s), ("ac", m1_bank_free.get(bank, 0)),
                         ("ac", bank_free.get(bank, 0))]
                    pe_m1 = None
                    for c in range(NEC):
                        pe_m1 = p.op("tensor", lambda e, sb_=sb_, b=b, c=c, bank=bank: e.matmul(
                            out=py[:, bank * 512:(bank + 1) * 512],
                            lhsT=w1s[sb_][:, c * 128:(c + 1) * 128],
                            rhs=xn_sb[b][:, c * 512:(c + 1) * 512],
                            start=(c == 0), stop=(c == NEC - 1)),
                             wait=w if c == 0 else (),
                             inc="pe" if c == NEC - 1 else None)
                    pe_m1_of[dc] = pe_m1
                    slab_pe[("w1", sb_)] = pe_m1
                    if dc >= 1:
                        bprev = 2 + ((dc - 1) % 3)
                        m1_bank_free[bprev] = p.op("scalar",
                            lambda e, e_=e_, dc=dc - 1, bank=bprev: e.activation(
                                out=h_sb[:, dc * 512:(dc + 1) * 512],
                                in_=py[:, bank * 512:(bank + 1) * 512],
                                func=AF.Relu, scale=0.5,
                                bias=b1_sb[:, e_ * 32 + dc:e_ * 32 + dc + 1]),
                            wait=[("pe", pe_m1)], inc="ac")
                bdum = 2 + (NDC % 3)
                pe_dum = None
                for c in range(2):
                    pe_dum = p.op("tensor", lambda e, b=b, c=c, bdum=bdum: e.matmul(
                        out=py[:, bdum * 512:(bdum + 1) * 512],
                        lhsT=e0b[:], rhs=b2_pad[:, 0:512],
                        start=(c == 0), stop=(c == 1)),
                         wait=[("ac", m1_bank_free.get(bdum, 0))] if c == 0 else (),
                         inc="pe" if c == 1 else None)
                blast = 2 + ((NDC - 1) % 3)
                m1_bank_free[blast] = p.op("scalar",
                    lambda e, e_=e_, dc=NDC - 1, bank=blast: e.activation(
                        out=h_sb[:, dc * 512:(dc + 1) * 512],
                        in_=py[:, bank * 512:(bank + 1) * 512],
                        func=AF.Relu, scale=0.5,
                        bias=b1_sb[:, e_ * 32 + dc:e_ * 32 + dc + 1]),
                    wait=[("pe", pe_dum)], inc="ac")
                m1_bank_free[bdum] = m1_bank_free[blast]
                ac_relu = flushb("scalar")
                if not m2_on:
                    continue
                for dc in range(NDC):
                    sb_ = dc % 2
                    p.dma("sync", w2s[sb_][:], w2bf_d[e_, dc],
                          wait=[("sq", w2bf_done.get((e_, dc), 0)),
                                ("pe", slab_pe.get(("w2", sb_), 0))])
                    ld_s = p.cnt["sq"]
                    w = [("sq", ld_s), ("ac", ac_relu)]
                    if dc == 0:
                        w += [("ac", bank_free.get(k2, 0)) for k2 in range(8)]
                    for sub in range(4):
                        for eo in range(2):
                            bank = sub * 2 + eo
                            slab_pe[("w2", sb_)] = p.op("tensor",
                                lambda e, sb_=sb_, dc=dc, sub=sub, eo=eo,
                                 bank=bank: e.matmul(
                                     out=py[:, bank * 512:(bank + 1) * 512],
                                     lhsT=h_sb[:, dc * 512 + sub * 128:
                                               dc * 512 + (sub + 1) * 128],
                                     rhs=w2s[sb_][:, eo * 512:(eo + 1) * 512],
                                     start=(dc == 0), stop=False),
                                 wait=w if (sub == 0 and eo == 0) else (),
                                 inc="pe" if (sub == 3 and eo == 1) else None)
                pe_y = None
                for sub in range(4):
                    for eo in range(2):
                        bank = sub * 2 + eo
                        pe_y = p.op("tensor", lambda e, e_=e_, eo=eo, bank=bank: e.matmul(
                            out=py[:, bank * 512:(bank + 1) * 512], lhsT=e0b[:],
                            rhs=b2_pad[:, e_ * E + eo * 512:e_ * E + eo * 512 + 512],
                            start=False, stop=True), (), inc="pe")
                pe_gate = pe_y
                yev_flush = []
                for sub in range(4):
                    for eo in range(2):
                        bank = sub * 2 + eo
                        col = (tt * 4 + sub) * NE + e_
                        bank_free[bank] = p.op("scalar", lambda e, bank=bank, col=col:
                            e.activation(out=evy[bank][:],
                                         in_=py[:, bank * 512:(bank + 1) * 512],
                                         func=AF.Identity, scale=wroute[:, col:col + 1]),
                            wait=[("pe", pe_gate), ("gp", evy_free[bank])], inc="ac")
                        if bank == 0:
                            pe_gate = pe_lag(py[:, 0:512], wait=[("ac", bank_free[0])])
                        yev_flush.append(flushb("scalar"))
                for _ in range(2):
                    p.op("scalar", lambda e: e.activation(out=tailsb[:], in_=masks[:, 0:512],
                                                          func=AF.Identity, scale=1.0))
                    yev_flush.append(flushb("scalar"))
                for sub in range(4):
                    for eo in range(2):
                        bank = sub * 2 + eo
                        p.op("gpsimd", lambda e, b=b, sub=sub, eo=eo, bank=bank:
                            e.tensor_tensor(
                                out=acc[b][:, sub * E + eo * 512:sub * E + eo * 512 + 512],
                                in0=evy[bank][:],
                                in1=acc[b][:, sub * E + eo * 512:sub * E + eo * 512 + 512],
                                op=OP.add), wait=[("ac", yev_flush[bank + 2])], inc="gp")
                p.op("gpsimd", lambda e: e.tensor_copy(out=tailsb[:], in_=bdbc[:, 0:512]),
                     inc="gp")
                gfree = p.cnt["gp"]
                for j in range(8):
                    evy_free[j] = gfree
            ck_tt_pe[tt] = p.cnt["pe"]
            for _ in range(16):
                p.op("gpsimd", lambda e: e.tensor_copy(out=tailsb[:], in_=bdbc[:, 0:512]))
            for sub in range(4):
                p.dma("gpsimd", out_t[tt * 512 + sub * 128:tt * 512 + (sub + 1) * 128, :],
                      acc[b][:, sub * E:(sub + 1) * E])
            ck_acc_gq[b] = p.cnt["gq"]

        if debug and dbg_dump:
            allw = [("sq", p.cnt["sq"]), ("pe", p.cnt["pe"]), ("dv", p.cnt["dv"]),
                    ("ac", p.cnt["ac"])]
            p.dma("sync", dbg_xnf, xnf_d, wait=allw)
            for t_ in range(NTT):
                for ec in range(NEC):
                    p.dma("sync", dbg_q[t_, ec], qf_d[t_, ec])
                    p.dma("sync", dbg_k[t_, ec], kf_d[t_, ec])
            for qt_ in range(NQT):
                for hp_ in range(NEC):
                    p.dma("sync", dbg_ctx[qt_, hp_], ctx_d[qt_, hp_])
            p.dma("sync", dbg_route, wroute[:])
            p.dma("sync", dbg_aux, bvecn[:])

        p.op("sync", lambda e: e.nop(),
             wait=[("sq", p.cnt["sq"]), ("gq", p.cnt["gq"]), ("pe", p.cnt["pe"]),
                   ("dv", p.cnt["dv"]), ("ac", p.cnt["ac"]), ("po", p.cnt["po"]),
                   ("aq", p.cnt.get("aq", 0))])

        sems = {name: es.enter_context(nc.semaphore(name)) for name in p.cnt}
        block = es.enter_context(nc.Block())
        p.emit(block, sems)
    return nc


_CONST_CACHE = {}


def _consts():
    if _CONST_CACHE:
        return _CONST_CACHE
    pos = np.arange(S, dtype=np.float64)
    inv_freq = 1.0 / (10000.0 ** (np.arange(0, HD, 2, dtype=np.float64) / HD))
    freqs = np.outer(pos, inv_freq)
    cos = np.cos(freqs).astype(np.float32)
    sin = np.sin(freqs).astype(np.float32)
    cossin = np.concatenate([cos, sin[:, 0:16], sin[:, 16:32]], axis=1)
    idm = np.eye(128, dtype=np.float32)
    e0 = np.zeros((128, 128), dtype=np.float32)
    e0[0, :] = 1.0
    ii = np.arange(128)[:, None]
    jj = np.arange(512)[None, :]
    masks = np.concatenate(
        [(ii <= jj - 128 * v).astype(np.float32) for v in range(4)], axis=1)
    _CONST_CACHE.update(dict(
        cossin=cossin,
        idbf=idm.astype(ml_dtypes.bfloat16),
        idf=idm,
        e0b=e0.astype(ml_dtypes.bfloat16),
        masks=masks.astype(ml_dtypes.bfloat16),
    ))
    return _CONST_CACHE


def kernel(**inputs):
    c = _consts()
    x = np.ascontiguousarray(np.asarray(inputs["L0_attn_hidden_states"], dtype=np.float32))
    base = dict(
        wq=np.ascontiguousarray(np.asarray(inputs["L0_attn_Wq"], dtype=np.float32)),
        wk=np.ascontiguousarray(np.asarray(inputs["L0_attn_Wk"], dtype=np.float32)),
        wv=np.ascontiguousarray(np.asarray(inputs["L0_attn_Wv"], dtype=np.float32)),
        wd=np.ascontiguousarray(np.asarray(inputs["L0_attn_Wd"], dtype=np.float32)),
        bqkv=np.stack([np.asarray(inputs["L0_attn_bq"], dtype=np.float32),
                       np.asarray(inputs["L0_attn_bk"], dtype=np.float32),
                       np.asarray(inputs["L0_attn_bv"], dtype=np.float32)], axis=0),
        bd=np.asarray(inputs["L0_attn_bd"], dtype=np.float32).reshape(1, E),
        w1=np.ascontiguousarray(np.asarray(inputs["L0_ffn_W1"], dtype=np.float32)),
        b1=np.ascontiguousarray(np.asarray(inputs["L0_ffn_b1"], dtype=np.float32)
                                .reshape(NE, 32, 128)),
        w2=np.ascontiguousarray(np.asarray(inputs["L0_ffn_W2"], dtype=np.float32)),
        b2=np.ascontiguousarray(np.asarray(inputs["L0_ffn_b2"], dtype=np.float32)
                                .reshape(1, NE * E)),
        cossin=c["cossin"], idbf=c["idbf"], idf=c["idf"], e0b=c["e0b"],
        masks=c["masks"],
    )
    nc = build()
    in_maps = [dict(base, x=np.ascontiguousarray(x[i])) for i in range(8)]
    res = run_bass_kernel_spmd(nc, in_maps, core_ids=list(range(8)))
    out = np.stack([np.asarray(r["out"]) for r in res.results], axis=0)
    return out.astype(np.float32)


if __name__ == "__main__":
    import time
    t0 = time.time()
    nc = build()
    print(f"built ok in {time.time() - t0:.1f}s")
